# version 01
# baseline (speedup 1.0000x reference)
import contextlib
import numpy as np
import ml_dtypes
import concourse.bass as bass
import concourse.mybir as mybir
from concourse.bass_utils import run_bass_kernel_spmd

F32 = mybir.dt.float32
BF16 = mybir.dt.bfloat16
AF = mybir.ActivationFunctionType
ALU = mybir.AluOpType
AX = mybir.AxisListType
bfloat16 = ml_dtypes.bfloat16

D = 2048
T = 2048
INTER = 8192
DC = 16
NT = 4
SEG = DC * 512
RMS_EPS = float(np.finfo(np.float32).eps)
LN_EPS = 1e-5
ENGINES = ("sync", "gpsimd", "tensor", "vector", "scalar")



class Sem:
    __slots__ = ("h", "n", "name")

    def __init__(self, h, name):
        self.h = h
        self.n = 0
        self.name = name


class Em:

    def __init__(self, nc, stack):
        self.nc = nc
        self.stack = stack
        self.tape = {e: [] for e in ENGINES}
        self.waited = {e: {} for e in ENGINES}

    def sem(self, name):
        h = self.stack.enter_context(self.nc.semaphore(name))
        return Sem(h, name)

    def wait(self, eng, sem, val):
        if sem is None or val is None or val <= 0:
            return
        if self.waited[eng].get(sem.name, 0) >= val:
            return
        self.waited[eng][sem.name] = val
        h = sem.h
        self.tape[eng].append(lambda e, h=h, v=val: e.wait_ge(h, v))

    def wait_evt(self, eng, evt):
        if evt is not None:
            self.wait(eng, evt[0], evt[1])

    def op(self, eng, meth, *args, inc=None, **kw):
        incs = []
        evt = None
        if inc:
            for s, k in inc:
                s.n += k
                incs.append((s.h, k))
                if evt is None:
                    evt = (s, s.n)

        def fn(e, meth=meth, args=args, kw=kw, incs=incs):
            ins = getattr(e, meth)(*args, **kw)
            for h, k in incs:
                ins.then_inc(h, k)
            return ins

        self.tape[eng].append(fn)
        return evt

    def replay(self):
        with self.nc.Block() as blk:
            @blk.sync
            def _(e):
                for f in self.tape["sync"]:
                    f(e)

            @blk.gpsimd
            def _(e):
                for f in self.tape["gpsimd"]:
                    f(e)

            @blk.tensor
            def _(e):
                for f in self.tape["tensor"]:
                    f(e)

            @blk.vector
            def _(e):
                for f in self.tape["vector"]:
                    f(e)

            @blk.scalar
            def _(e):
                for f in self.tape["scalar"]:
                    f(e)


class Banks:
    def __init__(self, tiles):
        self.tiles = tiles
        self.free = [None] * len(tiles)
        self.idx = 0

    def acquire(self):
        i = self.idx % len(self.tiles)
        self.idx += 1
        return i, self.tiles[i], self.free[i]

    def release(self, i, evt):
        self.free[i] = evt


class Ctx:
    pass


def make_ctx(nc, stack):
    em = Em(nc, stack)
    c = Ctx()
    c.nc = nc
    c.em = em
    c.dG = em.sem("dG")
    c.dS = em.sem("dS")
    c.tM = em.sem("tM")
    c.vV = em.sem("vV")
    c.sA = em.sem("sA")
    c.acc = Banks([stack.enter_context(nc.psum_tensor(f"acc{i}", [128, 512], F32))
                   for i in range(4)])
    c.ptr = Banks([stack.enter_context(nc.psum_tensor(f"ptr{i}", [128, 1024], BF16))
                   for i in range(2)])
    c.pstat = [stack.enter_context(nc.psum_tensor(f"pst{i}", [1, 512], F32))
               for i in range(2)]
    c.pstat_free = [None, None]
    c.idn = stack.enter_context(nc.sbuf_tensor("idn", [128, 128], BF16))
    c.ones_col = stack.enter_context(nc.sbuf_tensor("ones_col", [128, 1], BF16))
    c.ones_row = stack.enter_context(nc.sbuf_tensor("ones_row", [1, 128], BF16))
    c.epsc = stack.enter_context(nc.sbuf_tensor("epsc", [128, 2], F32))
    eps_d = nc.dram_tensor("EPSC", [128, 2], F32, kind="ExternalInput")
    em.op("sync", "dma_start", c.epsc[:, :], eps_d[:, :], inc=[(c.dS, 16)])
    idn_d = nc.dram_tensor("IDN", [128, 128], BF16, kind="ExternalInput")
    onc_d = nc.dram_tensor("ONESC", [128, 1], BF16, kind="ExternalInput")
    onr_d = nc.dram_tensor("ONESR", [1, 128], BF16, kind="ExternalInput")
    em.op("sync", "dma_start", c.idn[:, :], idn_d[:, :], inc=[(c.dS, 16)])
    em.op("sync", "dma_start", c.ones_col[:, :], onc_d[:, :], inc=[(c.dS, 16)])
    c.const_evt = em.op("sync", "dma_start", c.ones_row[:, :], onr_d[:, :],
                        inc=[(c.dS, 16)])
    return c


def const_inputs():
    return {
        "EPSC": np.tile(np.array([[RMS_EPS, LN_EPS]], np.float32), (128, 1)),
        "IDN": np.eye(128, dtype=np.float32).astype(bfloat16),
        "ONESC": np.ones((128, 1), np.float32).astype(bfloat16),
        "ONESR": np.ones((1, 128), np.float32).astype(bfloat16),
    }



def emit_wt_load(c, W_dram, wt_sb, wt_off, natbufs, nat_free, n_oslabs,
                 d_free, wt_free_evt=None, sign=False, sign_buf=None,
                 sign_free=None, row0=0):
    em = c.em
    nk = d_free // 128
    slab_ready = []
    prev_tr = None
    for s in range(n_oslabs):
        nbi = s % len(natbufs)
        nb = natbufs[nbi]
        em.wait_evt("gpsimd", nat_free[nbi])
        r0 = row0 + s * 128
        ld = em.op("gpsimd", "dma_start", nb[:, 0:d_free],
                   W_dram[r0:r0 + 128, 0:d_free], inc=[(c.dG, 16)])
        src = nb
        if sign:
            em.wait_evt("scalar", ld)
            em.wait_evt("scalar", sign_free[0])
            sg = em.op("scalar", "activation", sign_buf[:, 0:d_free],
                       nb[:, 0:d_free], AF.Sign, inc=[(c.sA, 1)])
            nat_free[nbi] = sg
            ld = sg
            src = sign_buf
        last_evac = None
        tev = None
        for g in range(nk // 4):
            bi, bt, bfree = c.ptr.acquire()
            em.wait_evt("tensor", ld)
            em.wait_evt("tensor", bfree)
            for i in range(4):
                k = g * 4 + i
                inc = [(c.tM, 1)] if i == 3 else None
                tev = em.op("tensor", "transpose", bt[:, i * 128:(i + 1) * 128],
                            src[:, k * 128:(k + 1) * 128], c.idn[:, :], inc=inc)
            em.wait_evt("vector", tev)
            em.wait_evt("vector", wt_free_evt)
            off = wt_off + s * d_free + g * 4 * 128
            last_evac = em.op("vector", "tensor_copy",
                              wt_sb[:, off:off + 512], bt[:, 0:512],
                              inc=[(c.vV, 1)])
            c.ptr.release(bi, last_evac)
        if sign:
            sign_free[0] = tev
        else:
            nat_free[nbi] = tev
        slab_ready.append(last_evac)
    return slab_ready



def make_ln_scratch(nc, stack, tag):
    a = stack.enter_context(nc.sbuf_tensor(f"ln_{tag}_a", [128, 2048], F32))
    cc = stack.enter_context(nc.sbuf_tensor(f"ln_{tag}_c", [128, 1024], BF16))
    s = {
        "m": a[0:1, 0:512], "rstd": a[0:1, 512:1024],
        "t0": a[0:1, 1024:1536], "t1": a[0:1, 1536:2048],
        "row_a": cc[0:1, 0:512], "row_b": cc[0:1, 512:1024],
    }
    return s


def emit_stats_mms(c, y_ap, y_evt, sq_buf, sq_free, first, last):
    em = c.em
    em.wait_evt("scalar", y_evt)
    em.wait_evt("scalar", sq_free)
    sqe = em.op("scalar", "activation", sq_buf[:, :], y_ap, AF.Square,
                inc=[(c.sA, 1)])
    if first:
        em.wait_evt("tensor", c.pstat_free[0])
        em.wait_evt("tensor", c.pstat_free[1])
    em.wait_evt("tensor", y_evt)
    em.op("tensor", "matmul", c.pstat[0][:, :], c.ones_col[:, :], y_ap,
          start=first, stop=last)
    em.wait_evt("tensor", sqe)
    ev = em.op("tensor", "matmul", c.pstat[1][:, :], c.ones_col[:, :],
               sq_buf[:, :], start=first, stop=last, inc=[(c.tM, 1)])
    return ev, sqe


def emit_ln_finalize(c, ln_sc, stats_evt, eps_col):
    em = c.em
    s = ln_sc
    em.wait_evt("vector", stats_evt)
    ev_m = em.op("vector", "tensor_scalar", s["m"], c.pstat[0][:, :],
                 1.0 / D, None, ALU.mult, inc=[(c.vV, 1)])
    em.op("vector", "tensor_tensor", s["t0"], s["m"], s["m"], ALU.mult)
    ev_var = em.op("vector", "scalar_tensor_tensor", s["t1"], c.pstat[1][:, :],
                   1.0 / D, s["t0"], ALU.mult, ALU.subtract, inc=[(c.vV, 1)])
    c.pstat_free[0] = ev_m
    c.pstat_free[1] = ev_var
    em.wait_evt("scalar", ev_var)
    em.op("scalar", "activation", s["t0"], s["t1"], AF.Ln,
          bias=c.epsc[0:1, eps_col:eps_col + 1])
    ev_r = em.op("scalar", "activation", s["rstd"], s["t0"], AF.Exp,
                 scale=-0.5, inc=[(c.sA, 1)])
    em.wait_evt("vector", ev_r)
    evA = em.op("vector", "tensor_copy", s["row_a"], s["rstd"],
                inc=[(c.vV, 1)])
    em.op("vector", "scalar_tensor_tensor", s["t1"], s["m"], -1.0,
          s["rstd"], ALU.mult, ALU.mult)
    evB = em.op("vector", "tensor_copy", s["row_b"], s["t1"],
                inc=[(c.vV, 1)])
    iA, Ab, fA = c.acc.acquire()
    em.wait_evt("tensor", fA)
    em.wait_evt("tensor", evA)
    evtA = em.op("tensor", "matmul", Ab[:, :], c.ones_row[:, :],
                 s["row_a"], start=True, stop=True, inc=[(c.tM, 1)])
    iB, Bb, fB = c.acc.acquire()
    em.wait_evt("tensor", fB)
    em.wait_evt("tensor", evB)
    evtB = em.op("tensor", "matmul", Bb[:, :], c.ones_row[:, :],
                 s["row_b"], start=True, stop=True, inc=[(c.tM, 1)])
    return (iA, Ab, evtA), (iB, Bb, evtB)



def build_launch_A():
    nc = bass.Bass("TRN2", target_bir_lowering=False, debug=False)
    stack = contextlib.ExitStack()

    xT = nc.dram_tensor("xT", [D, T], F32, kind="ExternalInput")
    Wq = nc.dram_tensor("Wq", [D, D], F32, kind="ExternalInput")
    Wk = nc.dram_tensor("Wk", [D, D], F32, kind="ExternalInput")
    Wv = nc.dram_tensor("Wv", [D, D], F32, kind="ExternalInput")
    Wg = nc.dram_tensor("Wg", [D, D], F32, kind="ExternalInput")
    Wp = nc.dram_tensor("Wp", [D, D], F32, kind="ExternalInput")
    LMASK = nc.dram_tensor("LMASK", [4 * 128, 512], BF16, kind="ExternalInput")
    x1_out = nc.dram_tensor("x1", [D, T], BF16, kind="ExternalOutput")

    q_scr = nc.dram_tensor("q_scr", [4 * D, 512], BF16, kind="Internal")
    k_scr = nc.dram_tensor("k_scr", [4 * D, 512], BF16, kind="Internal")
    g_scr = nc.dram_tensor("g_scr", [4 * D, 512], BF16, kind="Internal")
    og_scr = nc.dram_tensor("og_scr", [4 * D, 512], BF16, kind="Internal")
    r_scr = nc.dram_tensor("r_scr", [1, T], F32, kind="Internal")

    with stack:
        c = make_ctx(nc, stack)
        em = c.em

        AR_W = stack.enter_context(nc.sbuf_tensor("AR_W", [128, DC * D], BF16))
        AR_X = stack.enter_context(nc.sbuf_tensor("AR_X", [128, 2 * SEG], BF16))
        AR_V = stack.enter_context(nc.sbuf_tensor("AR_V", [128, DC * D], BF16))
        AR_B = stack.enter_context(nc.sbuf_tensor("AR_B", [128, SEG + 1024], BF16))
        nat = [stack.enter_context(nc.sbuf_tensor(f"nat{i}", [128, D], BF16))
               for i in range(2)]
        r_bc = stack.enter_context(nc.sbuf_tensor("r_bc", [128, T], BF16))
        r_tok = stack.enter_context(nc.sbuf_tensor("r_tok", [128, DC], F32))
        scr = [stack.enter_context(nc.sbuf_tensor(f"scr{i}", [128, 512], F32))
               for i in range(2)]
        piece = [stack.enter_context(nc.sbuf_tensor(f"piece{i}", [128, 512], BF16))
                 for i in range(2)]
        ln_sc = make_ln_scratch(nc, stack, "A")

        sq = AR_V[:, 0:512]
        xblk = [AR_X[:, 0:SEG], AR_X[:, SEG:2 * SEG]]
        xblk_free = [None, None]
        nat_free = [None, None]
        scr_free = [None, None]
        piece_free = [None] * 2

        xT_re = xT[:, :].rearrange("(c p) n -> p c n", p=128)

        def load_xblk(nt, slot):
            em.wait_evt("gpsimd", xblk_free[slot])
            src = xT_re[:, :, nt * 512:(nt + 1) * 512]
            dst = xblk[slot].rearrange("p (c n) -> p c n", c=DC)
            return em.op("gpsimd", "dma_start", dst, src, inc=[(c.dG, 16)])

        em.wait_evt("tensor", c.const_evt)
        for nt in range(NT):
            lev = load_xblk(nt, nt % 2)
            sqe = None
            for ch in range(DC):
                ya = xblk[nt % 2][:, ch * 512:(ch + 1) * 512]
                em.wait_evt("scalar", lev)
                em.wait_evt("scalar", sqe)
                sqe = em.op("scalar", "activation", sq[:, :], ya, AF.Square,
                            inc=[(c.sA, 1)])
                if ch == 0:
                    em.wait_evt("tensor", c.pstat_free[1])
                em.wait_evt("tensor", sqe)
                sev = em.op("tensor", "matmul", c.pstat[1][:, :], c.ones_col[:, :],
                            sq[:, :], start=(ch == 0), stop=(ch == DC - 1),
                            inc=[(c.tM, 1)])
            xblk_free[nt % 2] = sqe
            em.wait_evt("vector", sev)
            ev_var = em.op("vector", "tensor_scalar", ln_sc["t1"],
                           c.pstat[1][:, :], 1.0 / D, None, ALU.mult,
                           inc=[(c.vV, 1)])
            c.pstat_free[1] = ev_var
            em.wait_evt("scalar", ev_var)
            em.op("scalar", "activation", ln_sc["t0"], ln_sc["t1"],
                  AF.Ln, bias=c.epsc[0:1, 0:1])
            ev_r = em.op("scalar", "activation", ln_sc["rstd"],
                         ln_sc["t0"], AF.Exp, scale=-0.5, inc=[(c.sA, 1)])
            em.wait_evt("vector", ev_r)
            evA = em.op("vector", "tensor_copy", ln_sc["row_a"],
                        ln_sc["rstd"], inc=[(c.vV, 1)])
            em.wait_evt("sync", ev_r)
            em.op("sync", "dma_start", r_scr[0:1, nt * 512:(nt + 1) * 512],
                  ln_sc["rstd"], inc=[(c.dS, 16)])
            ia, tA, fA = c.acc.acquire()
            em.wait_evt("tensor", fA)
            em.wait_evt("tensor", evA)
            bev = em.op("tensor", "matmul", tA[:, :], c.ones_row[:, :],
                        ln_sc["row_a"], start=True, stop=True,
                        inc=[(c.tM, 1)])
            em.wait_evt("vector", bev)
            cev = em.op("vector", "tensor_copy", r_bc[:, nt * 512:(nt + 1) * 512],
                        tA[:, :], inc=[(c.vV, 1)])
            c.acc.release(ia, cev)
        r_rows_done = (c.dS, c.dS.n)
        em.wait("gpsimd", *r_rows_done)
        rt_src = r_scr[:, :].rearrange("one (cc p) -> p (one cc)", p=128)
        rt_ev = em.op("gpsimd", "dma_start", r_tok[:, :], rt_src,
                      inc=[(c.dG, 16)], allow_slow_non_contiguous=True)

        def proj_standard(wt_view, slab_ready, spill, kind):
            nonlocal_last = [None]
            pi = [0]
            for nt in range(NT):
                lev = load_xblk(nt, nt % 2)
                for o in range(DC):
                    ia, acc, fr = c.acc.acquire()
                    em.wait_evt("tensor", fr)
                    em.wait_evt("tensor", slab_ready[o])
                    em.wait_evt("tensor", lev)
                    for k in range(DC):
                        mme = em.op(
                            "tensor", "matmul", acc[:, :],
                            wt_view[:, o * D + k * 128:o * D + (k + 1) * 128],
                            xblk[nt % 2][:, k * 512:(k + 1) * 512],
                            start=(k == 0), stop=(k == DC - 1),
                            inc=([(c.tM, 1)] if k == DC - 1 else None))
                    em.wait_evt("vector", mme)
                    si = o % 2
                    s0 = scr[si]
                    em.wait_evt("vector", scr_free[si])
                    rv = em.op("vector", "tensor_tensor", s0[:, :], acc[:, :],
                               r_bc[:, nt * 512:(nt + 1) * 512], ALU.mult,
                               inc=[(c.vV, 1)])
                    c.acc.release(ia, rv)
                    p = pi[0] % 2
                    pc = piece[p]
                    if kind == "silu":
                        em.wait_evt("scalar", rv)
                        em.wait_evt("scalar", piece_free[p])
                        pe = em.op("scalar", "activation", pc[:, :], s0[:, :],
                                   AF.Silu, inc=[(c.sA, 1)])
                        scr_free[si] = pe
                    else:
                        sit = (o + 1) % 2
                        em.wait_evt("scalar", rv)
                        em.wait_evt("scalar", scr_free[sit])
                        th = em.op("scalar", "activation", scr[sit][:, :],
                                   s0[:, :], AF.Tanh, scale=0.5, inc=[(c.sA, 1)])
                        scr_free[si] = th
                        em.wait_evt("vector", th)
                        em.wait_evt("vector", piece_free[p])
                        pe = em.op("vector", "tensor_scalar", pc[:, :],
                                   scr[sit][:, :], 0.5, 0.5, ALU.mult, ALU.add,
                                   inc=[(c.vV, 1)])
                        scr_free[sit] = pe
                    em.wait_evt("sync", pe)
                    de = em.op("sync", "dma_start",
                               spill[nt * D + o * 128:nt * D + (o + 1) * 128, :],
                               pc[:, :], inc=[(c.dS, 16)])
                    piece_free[p] = de
                    pi[0] += 1
                xblk_free[nt % 2] = mme
            return mme

        wt_free = None
        sr = emit_wt_load(c, Wq, AR_W, 0, nat, nat_free, DC, D, wt_free)
        wt_free = proj_standard(AR_W, sr, q_scr, "silu")
        sr = emit_wt_load(c, Wk, AR_W, 0, nat, nat_free, DC, D, wt_free)
        wt_free = proj_standard(AR_W, sr, k_scr, "silu")
        sr = emit_wt_load(c, Wg, AR_W, 0, nat, nat_free, DC, D, wt_free)
        wt_free = proj_standard(AR_W, sr, g_scr, "sigmoid")
        sr = emit_wt_load(c, Wv, AR_W, 0, nat, nat_free, DC, D, wt_free)

        em.wait_evt("vector", rt_ev)
        wtv_re = AR_W[:, :].rearrange("p (s x) -> p s x", s=DC)
        v_last_mm = None
        for m in range(4):
            for nb in range(NT):
                lev = load_xblk(nb, nb % 2)
                for ns in range(4):
                    nch = nb * 4 + ns
                    ia, acc, fr = c.acc.acquire()
                    em.wait_evt("tensor", fr)
                    em.wait_evt("tensor", sr[4 * m + 3])
                    em.wait_evt("tensor", lev)
                    for k in range(DC):
                        rhs = wtv_re[:, 4 * m:4 * m + 4, k * 128:(k + 1) * 128]
                        mme = em.op(
                            "tensor", "matmul", acc[:, :],
                            xblk[nb % 2][:, k * 512 + ns * 128:k * 512 + (ns + 1) * 128],
                            rhs, start=(k == 0), stop=(k == DC - 1),
                            inc=([(c.tM, 1)] if k == DC - 1 else None))
                    em.wait_evt("vector", mme)
                    si = ns % 2
                    em.wait_evt("vector", scr_free[si])
                    rv = em.op("vector", "tensor_scalar", scr[si][:, :], acc[:, :],
                               r_tok[:, nch:nch + 1], None, ALU.mult,
                               inc=[(c.vV, 1)])
                    c.acc.release(ia, rv)
                    em.wait_evt("scalar", rv)
                    ve = em.op("scalar", "activation",
                               AR_V[:, nch * D + m * 512:nch * D + (m + 1) * 512],
                               scr[si][:, :], AF.Silu, inc=[(c.sA, 1)])
                    scr_free[si] = ve
                xblk_free[nb % 2] = mme
                v_last_mm = mme
        V_ready = (c.sA, c.sA.n)

        lmask = AR_W[:, 3 * SEG:3 * SEG + 2048]
        em.wait_evt("sync", (c.tM, c.tM.n))
        lm_dst = lmask[:, :].rearrange("p (m n) -> p m n", m=4)
        lm_src = LMASK[:, :].rearrange("(m p) n -> p m n", p=128)
        lm_ev = em.op("sync", "dma_start", lm_dst, lm_src, inc=[(c.dS, 16)])
        a1_spill_done = (c.dS, c.dS.n)

        kblk = [AR_W[:, 0:SEG], AR_W[:, SEG:2 * SEG]]
        S_sb = AR_W[:, 2 * SEG:3 * SEG]
        qblk = xblk
        gblk = AR_B[:, 0:SEG]
        og_sb = [AR_B[:, SEG + i * 512:SEG + (i + 1) * 512] for i in range(2)]

        def load_blk(scr_d, dst, t, ev_w, eng, sem):
            em.wait_evt(eng, ev_w)
            src = scr_d[t * D:(t + 1) * D, :].rearrange("(c p) n -> p c n", p=128)
            dst3 = dst.rearrange("p (c n) -> p c n", c=DC)
            return em.op(eng, "dma_start", dst3, src, inc=[(sem, 16)])

        kblk_free = [None, None]
        gblk_free = None
        og_free = [None, None]
        qblk_free = [xblk_free[0], xblk_free[1]]
        Ssb_free = None
        ogi = 0
        for t in range(4):
            em.wait("sync", *a1_spill_done)
            em.wait("gpsimd", *a1_spill_done)
            qlev = load_blk(q_scr, qblk[t % 2], t, qblk_free[t % 2], "sync", c.dS)
            glev = load_blk(g_scr, gblk, t, gblk_free, "gpsimd", c.dG)
            nkc = 4 * t + 4
            ev_s_seg = [None] * nkc
            for u in range(t + 1):
                klev = load_blk(k_scr, kblk[u % 2], u, kblk_free[u % 2],
                                "sync", c.dS)
                for jj in range(4):
                    j = 4 * u + jj
                    ia, acc, fr = c.acc.acquire()
                    em.wait_evt("tensor", fr)
                    em.wait_evt("tensor", klev)
                    em.wait_evt("tensor", qlev)
                    for k in range(DC):
                        mme = em.op(
                            "tensor", "matmul", acc[:, :],
                            kblk[u % 2][:, k * 512 + jj * 128:k * 512 + (jj + 1) * 128],
                            qblk[t % 2][:, k * 512:(k + 1) * 512],
                            start=(k == 0), stop=(k == DC - 1),
                            inc=([(c.tM, 1)] if k == DC - 1 else None))
                    em.wait_evt("vector", mme)
                    em.wait_evt("vector", Ssb_free)
                    if u == t:
                        em.wait_evt("vector", lm_ev)
                        me = em.op("vector", "tensor_tensor",
                                   S_sb[:, j * 512:(j + 1) * 512], acc[:, :],
                                   lmask[:, jj * 512:(jj + 1) * 512], ALU.mult,
                                   inc=[(c.vV, 1)])
                    else:
                        me = em.op("vector", "tensor_copy",
                                   S_sb[:, j * 512:(j + 1) * 512], acc[:, :],
                                   inc=[(c.vV, 1)])
                    c.acc.release(ia, me)
                    ev_s_seg[j] = me
                kblk_free[u % 2] = mme
            for dch in range(DC):
                ia, acc, fr = c.acc.acquire()
                em.wait_evt("tensor", fr)
                em.wait_evt("tensor", V_ready)
                for j in range(nkc):
                    em.wait_evt("tensor", ev_s_seg[j])
                    mme = em.op("tensor", "matmul", acc[:, :],
                                AR_V[:, j * D + dch * 128:j * D + (dch + 1) * 128],
                                S_sb[:, j * 512:(j + 1) * 512],
                                start=(j == 0), stop=(j == nkc - 1),
                                inc=([(c.tM, 1)] if j == nkc - 1 else None))
                em.wait_evt("vector", mme)
                em.wait_evt("vector", glev)
                em.wait_evt("vector", og_free[ogi % 2])
                ge = em.op("vector", "tensor_tensor", og_sb[ogi % 2][:, :],
                           acc[:, :], gblk[:, dch * 512:(dch + 1) * 512],
                           ALU.mult, inc=[(c.vV, 1)])
                c.acc.release(ia, ge)
                em.wait_evt("sync", ge)
                og_free[ogi % 2] = em.op(
                    "sync", "dma_start",
                    og_scr[t * D + dch * 128:t * D + (dch + 1) * 128, :],
                    og_sb[ogi % 2][:, :], inc=[(c.dS, 16)])
                ogi += 1
            qblk_free[t % 2] = mme
            gblk_free = ge
            Ssb_free = mme
        xblk_free[0] = qblk_free[0]
        xblk_free[1] = qblk_free[1]

        a2_done = (c.dS, c.dS.n)
        sr = emit_wt_load(c, Wp, AR_W, 0, nat, nat_free, DC, D, Ssb_free)
        ogblk = AR_X[:, 0:SEG]
        xres = AR_X[:, SEG:2 * SEG]
        Y = AR_B[:, 0:SEG]
        ogblk_free = qblk_free[0]
        xres_free = qblk_free[1]
        Y_free = gblk_free
        for nt in range(NT):
            em.wait("sync", *a2_done)
            olev = load_blk(og_scr, ogblk, nt, ogblk_free, "sync", c.dS)
            em.wait_evt("gpsimd", xres_free)
            xr_dst = xres.rearrange("p (cc n) -> p cc n", cc=DC)
            xlev = em.op("gpsimd", "dma_start", xr_dst,
                         xT_re[:, :, nt * 512:(nt + 1) * 512], inc=[(c.dG, 16)])
            stats_evt = None
            sq_evt = None
            for o in range(DC):
                ia, acc, fr = c.acc.acquire()
                em.wait_evt("tensor", fr)
                em.wait_evt("tensor", sr[o])
                em.wait_evt("tensor", olev)
                for k in range(DC):
                    mme = em.op("tensor", "matmul", acc[:, :],
                                AR_W[:, o * D + k * 128:o * D + (k + 1) * 128],
                                ogblk[:, k * 512:(k + 1) * 512],
                                start=(k == 0), stop=(k == DC - 1),
                                inc=([(c.tM, 1)] if k == DC - 1 else None))
                em.wait_evt("vector", mme)
                em.wait_evt("vector", xlev)
                em.wait_evt("vector", Y_free)
                ye = em.op("vector", "tensor_tensor", Y[:, o * 512:(o + 1) * 512],
                           acc[:, :], xres[:, o * 512:(o + 1) * 512], ALU.add,
                           inc=[(c.vV, 1)])
                c.acc.release(ia, ye)
                stats_evt, sq_evt = emit_stats_mms(
                    c, Y[:, o * 512:(o + 1) * 512], ye, sq, sq_evt,
                    first=(o == 0), last=(o == DC - 1))
            ogblk_free = mme
            xres_free = ye
            A_t, B_t = emit_ln_finalize(c, ln_sc, stats_evt, 1)
            iA, Ab, evtA = A_t
            iB, Bb, evtB = B_t
            pe = None
            for o in range(DC):
                em.wait_evt("vector", evtA)
                em.wait_evt("vector", evtB)
                si = o % 2
                em.wait_evt("vector", scr_free[si])
                em.op("vector", "tensor_tensor", scr[si][:, :],
                      Y[:, o * 512:(o + 1) * 512], Ab[:, :], ALU.mult)
                p = o % 2
                em.wait_evt("vector", piece_free[p])
                pe = em.op("vector", "tensor_tensor", piece[p][:, :],
                           scr[si][:, :], Bb[:, :], ALU.add, inc=[(c.vV, 1)])
                scr_free[si] = pe
                em.wait_evt("sync", pe)
                de = em.op("sync", "dma_start",
                           x1_out[o * 128:(o + 1) * 128, nt * 512:(nt + 1) * 512],
                           piece[p][:, :], inc=[(c.dS, 16)])
                piece_free[p] = de
            Y_free = pe
            c.acc.release(iA, pe)
            c.acc.release(iB, pe)

        em.wait("sync", c.dS, c.dS.n)
        em.wait("gpsimd", c.dG, c.dG.n)
        em.replay()
    return nc


def la_masks():
    m = np.zeros((4 * 128, 512), np.float32)
    kk = np.arange(128)[:, None]
    qq = np.arange(512)[None, :]
    for mi in range(4):
        m[mi * 128:(mi + 1) * 128] = (kk <= qq - 128 * mi)
    return m.astype(bfloat16)



IC = 64


def build_launch_B():
    nc = bass.Bass("TRN2", target_bir_lowering=False, debug=False)
    stack = contextlib.ExitStack()

    x1 = nc.dram_tensor("x1", [D, T], BF16, kind="ExternalInput")
    Wup = nc.dram_tensor("Wup", [INTER, D], F32, kind="ExternalInput")
    Wgt = nc.dram_tensor("Wgt", [INTER, D], F32, kind="ExternalInput")
    Wdn = nc.dram_tensor("Wdn", [D, INTER], F32, kind="ExternalInput")
    H2048 = nc.dram_tensor("H2048", [D, D], BF16, kind="ExternalInput")
    H128 = nc.dram_tensor("H128", [128, 128], BF16, kind="ExternalInput")
    SUU = nc.dram_tensor("SUU", [128, DC], F32, kind="ExternalInput")
    SUD = nc.dram_tensor("SUD", [128, IC], F32, kind="ExternalInput")
    WSU = nc.dram_tensor("WSU", [128, IC], F32, kind="ExternalInput")
    WSG = nc.dram_tensor("WSG", [128, IC], F32, kind="ExternalInput")
    WSD = nc.dram_tensor("WSD", [128, DC], F32, kind="ExternalInput")
    x2r_out = nc.dram_tensor("x2r", [D, T], BF16, kind="ExternalOutput")

    h_scr = nc.dram_tensor("h_scr", [4 * INTER, 512], BF16, kind="Internal")
    wdT_scr = nc.dram_tensor("wdT_scr", [DC * INTER, 128], BF16, kind="Internal")

    S1 = 1.0 / (64.0 * float(np.sqrt(2048.0)))
    S2 = 1.0 / (64.0 * float(np.sqrt(8192.0)))

    with stack:
        c = make_ctx(nc, stack)
        em = c.em

        AR_H = stack.enter_context(nc.sbuf_tensor("AR_H", [128, DC * D], BF16))
        AR_X2 = stack.enter_context(nc.sbuf_tensor("AR_X2", [128, DC * D], BF16))
        AR_X = stack.enter_context(nc.sbuf_tensor("AR_X", [128, 2 * SEG], BF16))
        h128_sb = stack.enter_context(nc.sbuf_tensor("h128", [128, 128], BF16))
        vecs = stack.enter_context(nc.sbuf_tensor("vecs", [128, 8 * IC], F32))
        suu_sb = vecs[:, 0:DC]
        sud_sb = vecs[:, IC:IC + IC]
        wsu_sb = vecs[:, 2 * IC:3 * IC]
        wsg_sb = vecs[:, 3 * IC:4 * IC]
        wsd_sb = vecs[:, 4 * IC:4 * IC + DC]
        wsu_su = vecs[:, 5 * IC:6 * IC]
        wsg64 = vecs[:, 6 * IC:7 * IC]
        wsd64 = vecs[:, 7 * IC:7 * IC + DC]
        scr = [stack.enter_context(nc.sbuf_tensor(f"scr{i}", [128, 512], F32))
               for i in range(2)]
        piece = [stack.enter_context(nc.sbuf_tensor(f"piece{i}", [128, 512], BF16))
                 for i in range(2)]
        ln_sc = make_ln_scratch(nc, stack, "B")
        sq = AR_X2[:, 25600:26112]

        scr_free = [None, None]
        piece_free = [None, None]

        for v_sb, v_d in ((suu_sb, SUU), (sud_sb, SUD), (wsu_sb, WSU),
                          (wsg_sb, WSG), (wsd_sb, WSD)):
            vev = em.op("sync", "dma_start", v_sb, v_d[:, :], inc=[(c.dS, 16)])
        em.op("sync", "dma_start", h128_sb[:, :], H128[:, :], inc=[(c.dS, 16)])
        em.wait_evt("vector", vev)
        em.wait("vector", c.dS, c.dS.n)
        em.op("vector", "tensor_tensor", wsu_su, wsu_sb, sud_sb, ALU.mult)
        vv1 = em.op("vector", "tensor_scalar", wsu_su, wsu_su, 64.0, None,
                    ALU.mult, inc=[(c.vV, 1)])
        vv2 = em.op("vector", "tensor_scalar", wsg64, wsg_sb, 64.0, None,
                    ALU.mult, inc=[(c.vV, 1)])
        vv3 = em.op("vector", "tensor_scalar", wsd64, wsd_sb, 64.0, None,
                    ALU.mult, inc=[(c.vV, 1)])
        H_re = H2048[:, :].rearrange("(c p) n -> p c n", p=128)
        hlev = em.op("sync", "dma_start",
                     AR_H[:, :].rearrange("p (c n) -> p c n", c=DC), H_re,
                     inc=[(c.dS, 16)])

        x1blk = AR_X[:, 0:SEG]
        xsblk = AR_X[:, SEG:2 * SEG]
        x1_re = x1[:, :].rearrange("(c p) n -> p c n", p=128)
        x1blk_free = None
        xs_free = None
        for nt in range(NT):
            em.wait_evt("gpsimd", x1blk_free)
            lev = em.op("gpsimd", "dma_start",
                        x1blk.rearrange("p (c n) -> p c n", c=DC),
                        x1_re[:, :, nt * 512:(nt + 1) * 512], inc=[(c.dG, 16)])
            sev = None
            for ch in range(DC):
                em.wait_evt("vector", lev)
                em.wait_evt("vector", xs_free)
                sev = em.op("vector", "tensor_scalar",
                            xsblk[:, ch * 512:(ch + 1) * 512],
                            x1blk[:, ch * 512:(ch + 1) * 512],
                            suu_sb[:, ch:ch + 1], None, ALU.mult,
                            inc=[(c.vV, 1)])
            x1blk_free = sev
            for o in range(DC):
                ia, acc, fr = c.acc.acquire()
                em.wait_evt("tensor", fr)
                em.wait_evt("tensor", hlev)
                em.wait_evt("tensor", sev)
                for k in range(DC):
                    mme = em.op("tensor", "matmul", acc[:, :],
                                AR_H[:, k * D + o * 128:k * D + (o + 1) * 128],
                                xsblk[:, k * 512:(k + 1) * 512],
                                start=(k == 0), stop=(k == DC - 1),
                                inc=([(c.tM, 1)] if k == DC - 1 else None))
                em.wait_evt("scalar", mme)
                ce = em.op("scalar", "activation",
                           AR_X2[:, o * D + nt * 512:o * D + (nt + 1) * 512],
                           acc[:, :], AF.Copy, scale=S1, inc=[(c.sA, 1)])
                c.acc.release(ia, ce)
            xs_free = mme
        x2_ready = (c.sA, c.sA.n)
        ar_h_free = mme

        natU = [AR_X[:, i * D:(i + 1) * D] for i in range(2)]
        natG = [AR_X[:, (2 + i) * D:(3 + i) * D] for i in range(2)]
        wtU = [AR_X[:, (4 + i) * D:(5 + i) * D] for i in range(2)]
        wtG = [AR_X[:, (6 + i) * D:(7 + i) * D] for i in range(2)]
        natU_free = [x1blk_free, x1blk_free]
        natG_free = [x1blk_free, x1blk_free]
        wtU_free = [xs_free, xs_free]
        wtG_free = [xs_free, xs_free]

        def slab_load(W_dram, cc, natb, nat_free_l, slot, wt_dst, wt_free_l):
            em.wait_evt("gpsimd", nat_free_l[slot])
            ld = em.op("gpsimd", "dma_start", natb[slot],
                       W_dram[cc * 128:(cc + 1) * 128, :], inc=[(c.dG, 16)])
            tev = None
            evac = None
            for g in range(4):
                bi, bt, bfree = c.ptr.acquire()
                em.wait_evt("tensor", ld)
                em.wait_evt("tensor", bfree)
                for i in range(4):
                    k = g * 4 + i
                    inc = [(c.tM, 1)] if i == 3 else None
                    tev = em.op("tensor", "transpose",
                                bt[:, i * 128:(i + 1) * 128],
                                natb[slot][:, k * 128:(k + 1) * 128],
                                c.idn[:, :], inc=inc)
                em.wait_evt("vector", tev)
                em.wait_evt("vector", wt_free_l[slot])
                evac = em.op("vector", "tensor_copy",
                             wt_dst[slot][:, g * 512:(g + 1) * 512],
                             bt[:, 0:512], inc=[(c.vV, 1)])
                c.ptr.release(bi, evac)
            nat_free_l[slot] = tev
            return evac

        hp_free = [None, None]
        for cc in range(IC):
            sl = cc % 2
            uev = slab_load(Wup, cc, natU, natU_free, sl, wtU, wtU_free)
            gev = slab_load(Wgt, cc, natG, natG_free, sl, wtG, wtG_free)
            for nt in range(NT):
                iu, pu, fu = c.acc.acquire()
                em.wait_evt("tensor", fu)
                em.wait_evt("tensor", uev)
                em.wait_evt("tensor", x2_ready)
                for k in range(DC):
                    mmu = em.op("tensor", "matmul", pu[:, :],
                                wtU[sl][:, k * 128:(k + 1) * 128],
                                AR_X2[:, k * D + nt * 512:k * D + (nt + 1) * 512],
                                start=(k == 0), stop=(k == DC - 1),
                                inc=([(c.tM, 1)] if k == DC - 1 else None))
                ig, pg, fg = c.acc.acquire()
                em.wait_evt("tensor", fg)
                em.wait_evt("tensor", gev)
                for k in range(DC):
                    mmg = em.op("tensor", "matmul", pg[:, :],
                                wtG[sl][:, k * 128:(k + 1) * 128],
                                AR_X2[:, k * D + nt * 512:k * D + (nt + 1) * 512],
                                start=(k == 0), stop=(k == DC - 1),
                                inc=([(c.tM, 1)] if k == DC - 1 else None))
                em.wait_evt("scalar", mmg)
                em.wait_evt("scalar", scr_free[0])
                sg = em.op("scalar", "activation", scr[0][:, :], pg[:, :],
                           AF.Silu, scale=wsg64[:, cc:cc + 1], inc=[(c.sA, 1)])
                c.acc.release(ig, sg)
                em.wait_evt("vector", mmu)
                em.wait_evt("vector", vv1)
                em.wait_evt("vector", scr_free[1])
                us = em.op("vector", "tensor_scalar", scr[1][:, :], pu[:, :],
                           wsu_su[:, cc:cc + 1], None, ALU.mult, inc=[(c.vV, 1)])
                c.acc.release(iu, us)
                p = nt % 2
                em.wait_evt("vector", sg)
                em.wait_evt("vector", hp_free[p])
                he = em.op("vector", "tensor_tensor", piece[p][:, :],
                           scr[0][:, :], scr[1][:, :], ALU.mult, inc=[(c.vV, 1)])
                scr_free[0] = he
                scr_free[1] = he
                em.wait_evt("sync", he)
                hp_free[p] = em.op(
                    "sync", "dma_start",
                    h_scr[nt * INTER + cc * 128:nt * INTER + (cc + 1) * 128, :],
                    piece[p][:, :], inc=[(c.dS, 16)])
            wtU_free[sl] = mmu
            wtG_free[sl] = mmg

        nat8 = [AR_X[:, 0:INTER], AR_X[:, INTER:2 * INTER]]
        nat8_free = [natU_free[0], natU_free[1]]
        wdp_free = [hp_free[0], hp_free[1]]
        b2_done = mmg
        for s in range(DC):
            sl = s % 2
            em.wait_evt("gpsimd", nat8_free[sl])
            em.wait("gpsimd", c.tM, c.tM.n)
            ld = em.op("gpsimd", "dma_start", nat8[sl],
                       Wdn[s * 128:(s + 1) * 128, :], inc=[(c.dG, 16)])
            for g in range(16):
                bi, bt, bfree = c.ptr.acquire()
                em.wait_evt("tensor", ld)
                em.wait_evt("tensor", bfree)
                for i in range(4):
                    k = g * 4 + i
                    inc = [(c.tM, 1)] if i == 3 else None
                    tev = em.op("tensor", "transpose",
                                bt[:, i * 128:(i + 1) * 128],
                                nat8[sl][:, k * 128:(k + 1) * 128],
                                c.idn[:, :], inc=inc)
                p = g % 2
                em.wait_evt("vector", tev)
                em.wait_evt("vector", wdp_free[p])
                ev = em.op("vector", "tensor_copy", piece[p][:, :], bt[:, 0:512],
                           inc=[(c.vV, 1)])
                c.ptr.release(bi, ev)
                em.wait_evt("sync", ev)
                dst = wdT_scr[s * INTER + g * 512:s * INTER + (g + 1) * 512, :]
                dst3 = dst.rearrange("(cc i) o -> i cc o", i=128)
                src3 = piece[p][:, :].rearrange("i (cc o) -> i cc o", cc=4)
                wdp_free[p] = em.op("sync", "dma_start", dst3, src3,
                                    inc=[(c.dS, 16)])
            nat8_free[sl] = tev
        piece_free = [wdp_free[0], wdp_free[1]]
        wd_prep_done = (c.dS, c.dS.n)

        hsup = AR_H
        wdT = [AR_X[:, i * INTER:(i + 1) * INTER] for i in range(2)]
        Y = AR_X2[:, 0:SEG]
        x1p = [AR_X2[:, SEG + i * 512:SEG + (i + 1) * 512] for i in range(2)]
        hsup_free = ar_h_free
        wdT_free = [nat8_free[0], nat8_free[1]]
        Y_free = b2_done
        x1p_free = [b2_done, b2_done]
        sq_evt0 = b2_done
        for sup in range(NT):
            em.wait("sync", *wd_prep_done)
            em.wait_evt("sync", hsup_free)
            hld = em.op("sync", "dma_start",
                        hsup[:, :].rearrange("p (cc n) -> p cc n", cc=IC),
                        h_scr[sup * INTER:(sup + 1) * INTER, :]
                        .rearrange("(cc p) n -> p cc n", p=128),
                        inc=[(c.dS, 16)])
            h2ev = None
            for cc in range(IC):
                ia, acc, fr = c.acc.acquire()
                em.wait_evt("tensor", fr)
                em.wait_evt("tensor", hld)
                mme = em.op("tensor", "matmul", acc[:, :], h128_sb[:, :],
                            hsup[:, cc * 512:(cc + 1) * 512],
                            start=True, stop=True, inc=[(c.tM, 1)])
                em.wait_evt("scalar", mme)
                h2ev = em.op("scalar", "activation",
                             hsup[:, cc * 512:(cc + 1) * 512], acc[:, :],
                             AF.Copy, scale=S2, inc=[(c.sA, 1)])
                c.acc.release(ia, h2ev)
            em.wait_evt("vector", h2ev)
            em.wait("vector", c.sA, c.sA.n)
            bev = None
            hh = 1
            while hh < IC:
                for blk in range(0, IC, 2 * hh):
                    for j2 in range(hh):
                        a_ap = hsup[:, (blk + j2) * 512:(blk + j2 + 1) * 512]
                        b_ap = hsup[:, (blk + hh + j2) * 512:(blk + hh + j2 + 1) * 512]
                        em.op("vector", "tensor_sub", b_ap, a_ap, b_ap)
                        bev = em.op("vector", "scalar_tensor_tensor", a_ap, a_ap,
                                    2.0, b_ap, ALU.mult, ALU.subtract,
                                    inc=[(c.vV, 1)])
                hh *= 2
            stats_evt = None
            sq_evt = sq_evt0
            for s in range(DC):
                sl = s % 2
                em.wait_evt("sync", wdT_free[sl])
                wld = em.op("sync", "dma_start",
                            wdT[sl].rearrange("i (cc o) -> i cc o", cc=IC),
                            wdT_scr[s * INTER:(s + 1) * INTER, :]
                            .rearrange("(cc i) o -> i cc o", i=128),
                            inc=[(c.dS, 16)])
                em.wait_evt("gpsimd", x1p_free[sl])
                x1ld = em.op("gpsimd", "dma_start", x1p[sl],
                             x1[s * 128:(s + 1) * 128,
                                sup * 512:(sup + 1) * 512], inc=[(c.dG, 16)])
                ia, acc, fr = c.acc.acquire()
                em.wait_evt("tensor", fr)
                em.wait_evt("tensor", wld)
                em.wait_evt("tensor", bev)
                for cc in range(IC):
                    mme = em.op("tensor", "matmul", acc[:, :],
                                wdT[sl][:, cc * 128:(cc + 1) * 128],
                                hsup[:, cc * 512:(cc + 1) * 512],
                                start=(cc == 0), stop=(cc == IC - 1),
                                inc=([(c.tM, 1)] if cc == IC - 1 else None))
                wdT_free[sl] = mme
                em.wait_evt("vector", mme)
                em.wait_evt("vector", x1ld)
                em.wait_evt("vector", Y_free)
                em.wait_evt("vector", vv3)
                ye = em.op("vector", "scalar_tensor_tensor",
                           Y[:, s * 512:(s + 1) * 512], acc[:, :],
                           wsd64[:, s:s + 1], x1p[sl], ALU.mult, ALU.add,
                           inc=[(c.vV, 1)])
                c.acc.release(ia, ye)
                x1p_free[sl] = ye
                stats_evt, sq_evt = emit_stats_mms(
                    c, Y[:, s * 512:(s + 1) * 512], ye, sq, sq_evt,
                    first=(s == 0), last=(s == DC - 1))
            hsup_free = mme
            A_t, B_t = emit_ln_finalize(c, ln_sc, stats_evt, 1)
            iA, Ab, evtA = A_t
            iB, Bb, evtB = B_t
            pe = None
            for s in range(DC):
                em.wait_evt("vector", evtA)
                em.wait_evt("vector", evtB)
                si = s % 2
                em.wait_evt("vector", scr_free[si])
                em.op("vector", "tensor_tensor", scr[si][:, :],
                      Y[:, s * 512:(s + 1) * 512], Ab[:, :], ALU.mult)
                p = s % 2
                em.wait_evt("vector", piece_free[p])
                pe = em.op("vector", "tensor_tensor", piece[p][:, :],
                           scr[si][:, :], Bb[:, :], ALU.add, inc=[(c.vV, 1)])
                scr_free[si] = pe
                em.wait_evt("sync", pe)
                de = em.op("sync", "dma_start",
                           x2r_out[s * 128:(s + 1) * 128,
                                   sup * 512:(sup + 1) * 512],
                           piece[p][:, :], inc=[(c.dS, 16)])
                piece_free[p] = de
            Y_free = pe
            c.acc.release(iA, pe)
            c.acc.release(iB, pe)

        em.wait("sync", c.dS, c.dS.n)
        em.wait("gpsimd", c.dG, c.dG.n)
        em.replay()
    return nc


def hadamard(n):
    H = np.array([[1.0]], dtype=np.float32)
    while H.shape[0] < n:
        H = np.block([[H, H], [H, -H]]).astype(np.float32)
    return H



ISQ = 1.0 / float(np.sqrt(128.0))


def build_launch_C(debug=False):
    nc = bass.Bass("TRN2", target_bir_lowering=False, debug=False)
    stack = contextlib.ExitStack()

    x2r = nc.dram_tensor("x2r", [D, T], BF16, kind="ExternalInput")
    Wq = nc.dram_tensor("Wq1", [D, D], F32, kind="ExternalInput")
    Wk = nc.dram_tensor("Wk1", [D, D], F32, kind="ExternalInput")
    Wv = nc.dram_tensor("Wv1", [D, D], F32, kind="ExternalInput")
    Wo = nc.dram_tensor("Wo1", [D, D], F32, kind="ExternalInput")
    AMASK = nc.dram_tensor("AMASK", [128, 128], F32, kind="ExternalInput")
    x3_out = nc.dram_tensor("x3", [D, T], BF16, kind="ExternalOutput")

    skind = "ExternalOutput" if debug else "Internal"
    q_scr = nc.dram_tensor("q1_scr", [4 * D, 512], BF16, kind=skind)
    k_scr = nc.dram_tensor("k1_scr", [4 * D, 512], BF16, kind=skind)
    o_scr = nc.dram_tensor("o1_scr", [4 * D, 512], BF16, kind=skind)
    if debug:
        S_dbg = nc.dram_tensor("S_dbg", [128, T], F32, kind="ExternalOutput")
        P_dbg = nc.dram_tensor("P_dbg", [128, T], BF16, kind="ExternalOutput")
        PT_dbg = nc.dram_tensor("PT_dbg", [128, DC * 512], BF16,
                                kind="ExternalOutput")
        R_dbg = nc.dram_tensor("R_dbg", [128, 4], F32, kind="ExternalOutput")

    with stack:
        c = make_ctx(nc, stack)
        em = c.em

        AR_W = stack.enter_context(nc.sbuf_tensor("AR_W", [128, DC * D], BF16))
        AR_X = stack.enter_context(nc.sbuf_tensor("AR_X", [128, 2 * SEG], BF16))
        AR_V = stack.enter_context(nc.sbuf_tensor("AR_V", [128, DC * D], BF16))
        AR_B = stack.enter_context(nc.sbuf_tensor("AR_B", [128, SEG + 1024], BF16))
        nat = [stack.enter_context(nc.sbuf_tensor("nat0", [128, D], BF16))]
        S_sb = stack.enter_context(nc.sbuf_tensor("S_sb", [128, T], F32))
        P_sb = AR_B[:, 0:T]
        amask = stack.enter_context(nc.sbuf_tensor("amask", [128, 128], F32))
        rowb = stack.enter_context(nc.sbuf_tensor("rowb", [128, 4], F32))
        scr = [stack.enter_context(nc.sbuf_tensor(f"scr{i}", [128, 512], F32))
               for i in range(2)]
        piece = [stack.enter_context(nc.sbuf_tensor(f"piece{i}", [128, 512], BF16))
                 for i in range(2)]
        ln_sc = make_ln_scratch(nc, stack, "C")
        sq = AR_V[:, 0:512]

        xblk = [AR_X[:, 0:SEG], AR_X[:, SEG:2 * SEG]]
        xblk_free = [None, None]
        nat_free = [None]
        scr_free = [None, None]
        piece_free = [None, None]

        amask_ev = em.op("sync", "dma_start", amask[:, :], AMASK[:, :],
                         inc=[(c.dS, 16)])
        x_re = x2r[:, :].rearrange("(c p) n -> p c n", p=128)

        def load_xblk(nt, slot):
            em.wait_evt("gpsimd", xblk_free[slot])
            src = x_re[:, :, nt * 512:(nt + 1) * 512]
            dst = xblk[slot].rearrange("p (c n) -> p c n", c=DC)
            return em.op("gpsimd", "dma_start", dst, src, inc=[(c.dG, 16)])

        em.wait_evt("tensor", c.const_evt)

        def proj_plain(wt_view, slab_ready, spill):
            pi = [0]
            for nt in range(NT):
                lev = load_xblk(nt, nt % 2)
                for o in range(DC):
                    ia, acc, fr = c.acc.acquire()
                    em.wait_evt("tensor", fr)
                    em.wait_evt("tensor", slab_ready[o])
                    em.wait_evt("tensor", lev)
                    for k in range(DC):
                        mme = em.op(
                            "tensor", "matmul", acc[:, :],
                            wt_view[:, o * D + k * 128:o * D + (k + 1) * 128],
                            xblk[nt % 2][:, k * 512:(k + 1) * 512],
                            start=(k == 0), stop=(k == DC - 1),
                            inc=([(c.tM, 1)] if k == DC - 1 else None))
                    p = pi[0] % 2
                    em.wait_evt("vector", mme)
                    em.wait_evt("vector", piece_free[p])
                    pe = em.op("vector", "tensor_copy", piece[p][:, :], acc[:, :],
                               inc=[(c.vV, 1)])
                    c.acc.release(ia, pe)
                    em.wait_evt("sync", pe)
                    de = em.op("sync", "dma_start",
                               spill[nt * D + o * 128:nt * D + (o + 1) * 128, :],
                               piece[p][:, :], inc=[(c.dS, 16)])
                    piece_free[p] = de
                    pi[0] += 1
                xblk_free[nt % 2] = mme
            return mme

        wt_free = None
        sr = emit_wt_load(c, Wq, AR_W, 0, nat, nat_free, DC, D, wt_free)
        wt_free = proj_plain(AR_W, sr, q_scr)
        sr = emit_wt_load(c, Wk, AR_W, 0, nat, nat_free, DC, D, wt_free)
        wt_free = proj_plain(AR_W, sr, k_scr)
        sr = emit_wt_load(c, Wv, AR_W, 0, nat, nat_free, DC, D, wt_free)

        wtv_re = AR_W[:, :].rearrange("p (s x) -> p s x", s=DC)
        for m in range(4):
            for nb in range(NT):
                lev = load_xblk(nb, nb % 2)
                for ns in range(4):
                    nch = nb * 4 + ns
                    ia, acc, fr = c.acc.acquire()
                    em.wait_evt("tensor", fr)
                    em.wait_evt("tensor", sr[4 * m + 3])
                    em.wait_evt("tensor", lev)
                    for k in range(DC):
                        rhs = wtv_re[:, 4 * m:4 * m + 4, k * 128:(k + 1) * 128]
                        mme = em.op(
                            "tensor", "matmul", acc[:, :],
                            xblk[nb % 2][:, k * 512 + ns * 128:k * 512 + (ns + 1) * 128],
                            rhs, start=(k == 0), stop=(k == DC - 1),
                            inc=([(c.tM, 1)] if k == DC - 1 else None))
                    em.wait_evt("scalar", mme)
                    ve = em.op("scalar", "activation",
                               AR_V[:, nch * D + m * 512:nch * D + (m + 1) * 512],
                               acc[:, :], AF.Copy, inc=[(c.sA, 1)])
                    c.acc.release(ia, ve)
                xblk_free[nb % 2] = mme
        V_ready = (c.sA, c.sA.n)
        v_mms_done = (c.tM, c.tM.n)
        c1_spill_done = (c.dS, c.dS.n)

        kblk = [AR_W[:, u * SEG:(u + 1) * SEG] for u in range(4)]
        qblk = xblk[0]
        PT_sb = AR_X[:, SEG:2 * SEG]
        PT3 = PT_sb.rearrange("p (j q) -> p j q", j=DC)
        nmax = rowb[:, 0:1]
        rsum = rowb[:, 1:2]
        rinv = rowb[:, 2:3]

        kblk_ld = [None] * 4
        qblk_free = xblk_free[0]
        pt_free = xblk_free[1]
        S_free = None
        P_free = None
        op_free = [piece_free[0], piece_free[1]]
        opi = 0
        for t in range(4):
            em.wait("sync", *c1_spill_done)
            em.wait_evt("sync", v_mms_done)
            em.wait_evt("sync", qblk_free)
            qlev = em.op("sync", "dma_start",
                         qblk.rearrange("p (c n) -> p c n", c=DC),
                         q_scr[t * D:(t + 1) * D, :]
                         .rearrange("(c p) n -> p c n", p=128),
                         inc=[(c.dS, 16)])
            kblk_ld[t] = em.op("sync", "dma_start",
                               kblk[t].rearrange("p (c n) -> p c n", c=DC),
                               k_scr[t * D:(t + 1) * D, :]
                               .rearrange("(c p) n -> p c n", p=128),
                               inc=[(c.dS, 16)])
            for h in range(16):
                for m2 in range(4):
                    lhsq = qblk[:, h * 512 + m2 * 128:h * 512 + (m2 + 1) * 128]
                    for u in range(t + 1):
                        ia, acc, fr = c.acc.acquire()
                        em.wait_evt("tensor", fr)
                        em.wait_evt("tensor", qlev)
                        em.wait_evt("tensor", kblk_ld[u])
                        wdt = 512 if u < t else (m2 + 1) * 128
                        mme = em.op("tensor", "matmul", acc[:, 0:wdt], lhsq,
                                    kblk[u][:, h * 512:h * 512 + wdt],
                                    start=True, stop=True, inc=[(c.tM, 1)])
                        em.wait_evt("vector", mme)
                        em.wait_evt("vector", S_free)
                        if u < t:
                            se = em.op("vector", "tensor_scalar",
                                       S_sb[:, u * 512:(u + 1) * 512],
                                       acc[:, 0:512], ISQ, None, ALU.mult,
                                       inc=[(c.vV, 1)])
                        else:
                            if m2 > 0:
                                em.op("vector", "tensor_scalar",
                                      S_sb[:, t * 512:t * 512 + m2 * 128],
                                      acc[:, 0:m2 * 128], ISQ, None, ALU.mult)
                            em.wait_evt("vector", amask_ev)
                            se = em.op("vector", "scalar_tensor_tensor",
                                       S_sb[:, t * 512 + m2 * 128:t * 512 + (m2 + 1) * 128],
                                       acc[:, m2 * 128:(m2 + 1) * 128], ISQ,
                                       amask[:, :], ALU.mult, ALU.add,
                                       inc=[(c.vV, 1)])
                        c.acc.release(ia, se)
                    W = t * 512 + (m2 + 1) * 128
                    nw = em.op("vector", "reduce_max", nmax, S_sb[:, 0:W],
                               axis=AX.X, negate=True, inc=[(c.vV, 1)])
                    em.wait_evt("scalar", nw)
                    em.wait_evt("scalar", P_free)
                    ex = em.op("scalar", "activation", P_sb[:, 0:W],
                               S_sb[:, 0:W], AF.Exp, bias=nmax,
                               inc=[(c.sA, 1)])
                    S_free = ex
                    em.wait_evt("vector", ex)
                    rse = em.op("vector", "reduce_sum", rsum, P_sb[:, 0:W],
                                axis=AX.X, inc=[(c.vV, 1)])
                    em.wait_evt("vector", rse)
                    rie = em.op("vector", "reciprocal", rinv, rsum,
                                inc=[(c.vV, 1)])
                    em.wait_evt("vector", rie)
                    pn = em.op("vector", "tensor_scalar", P_sb[:, 0:W],
                               P_sb[:, 0:W], rinv, None, ALU.mult,
                               inc=[(c.vV, 1)])
                    ntr = 4 * t + m2 + 1
                    tev = None
                    for g in range((ntr + 3) // 4):
                        gn = min(4, ntr - g * 4)
                        bi, bt, bfree = c.ptr.acquire()
                        em.wait_evt("tensor", pn)
                        em.wait_evt("tensor", bfree)
                        for i in range(gn):
                            j2 = g * 4 + i
                            inc = [(c.tM, 1)] if i == gn - 1 else None
                            tev = em.op("tensor", "transpose",
                                        bt[:, i * 128:(i + 1) * 128],
                                        P_sb[:, j2 * 128:(j2 + 1) * 128],
                                        c.idn[:, :], inc=inc)
                        em.wait_evt("vector", tev)
                        em.wait_evt("vector", pt_free)
                        ptv = em.op("vector", "tensor_copy",
                                    PT3[:, g * 4:g * 4 + gn,
                                        m2 * 128:(m2 + 1) * 128],
                                    bt[:, 0:gn * 128].rearrange(
                                        "p (j q) -> p j q", j=gn),
                                    inc=[(c.vV, 1)])
                        c.ptr.release(bi, ptv)
                    P_free = tev
                    for j2 in range(ntr, 4 * t + 4):
                        ptv = em.op("vector", "memset",
                                    PT3[:, j2:j2 + 1, m2 * 128:(m2 + 1) * 128],
                                    0.0, inc=[(c.vV, 1)])
                    ptw = ptv
                ia, acc, fr = c.acc.acquire()
                em.wait_evt("tensor", fr)
                em.wait_evt("tensor", V_ready)
                em.wait_evt("tensor", ptw)
                for j2 in range(4 * t + 4):
                    mme = em.op("tensor", "matmul", acc[:, :],
                                AR_V[:, j2 * D + h * 128:j2 * D + (h + 1) * 128],
                                PT_sb[:, j2 * 512:(j2 + 1) * 512],
                                start=(j2 == 0), stop=(j2 == 4 * t + 3),
                                inc=([(c.tM, 1)] if j2 == 4 * t + 3 else None))
                pt_free = mme
                p = opi % 2
                em.wait_evt("vector", mme)
                em.wait_evt("vector", op_free[p])
                oe = em.op("vector", "tensor_copy", piece[p][:, :], acc[:, :],
                           inc=[(c.vV, 1)])
                c.acc.release(ia, oe)
                em.wait_evt("sync", oe)
                op_free[p] = em.op("sync", "dma_start",
                                   o_scr[t * D + h * 128:t * D + (h + 1) * 128, :],
                                   piece[p][:, :], inc=[(c.dS, 16)])
                opi += 1
                if debug and t == 3 and h == 15:
                    em.wait_evt("sync", oe)
                    em.op("sync", "dma_start", S_dbg[:, :], S_sb[:, :],
                          inc=[(c.dS, 16)])
                    em.op("sync", "dma_start", P_dbg[:, :], P_sb[:, :],
                          inc=[(c.dS, 16)])
                    em.op("sync", "dma_start", PT_dbg[:, :], PT_sb[:, :],
                          inc=[(c.dS, 16)])
                    em.op("sync", "dma_start", R_dbg[:, :], rowb[:, :],
                          inc=[(c.dS, 16)])
            qblk_free = mme

        piece_free = [op_free[0], op_free[1]]
        c2_done = (c.dS, c.dS.n)

        sr = emit_wt_load(c, Wo, AR_W, 0, nat, nat_free, DC, D, mme)
        ogblk = AR_X[:, 0:SEG]
        xres = AR_X[:, SEG:2 * SEG]
        Y = AR_B[:, 0:SEG]
        ogblk_free = qblk_free
        xres_free = pt_free
        Y_free = P_free
        for nt in range(NT):
            em.wait("sync", *c2_done)
            em.wait_evt("sync", ogblk_free)
            olev = em.op("sync", "dma_start",
                         ogblk.rearrange("p (c n) -> p c n", c=DC),
                         o_scr[nt * D:(nt + 1) * D, :]
                         .rearrange("(c p) n -> p c n", p=128),
                         inc=[(c.dS, 16)])
            em.wait_evt("gpsimd", xres_free)
            xlev = em.op("gpsimd", "dma_start",
                         xres.rearrange("p (cc n) -> p cc n", cc=DC),
                         x_re[:, :, nt * 512:(nt + 1) * 512], inc=[(c.dG, 16)])
            stats_evt = None
            sq_evt = pt_free
            for o in range(DC):
                ia, acc, fr = c.acc.acquire()
                em.wait_evt("tensor", fr)
                em.wait_evt("tensor", sr[o])
                em.wait_evt("tensor", olev)
                for k in range(DC):
                    mme = em.op("tensor", "matmul", acc[:, :],
                                AR_W[:, o * D + k * 128:o * D + (k + 1) * 128],
                                ogblk[:, k * 512:(k + 1) * 512],
                                start=(k == 0), stop=(k == DC - 1),
                                inc=([(c.tM, 1)] if k == DC - 1 else None))
                em.wait_evt("vector", mme)
                em.wait_evt("vector", xlev)
                em.wait_evt("vector", Y_free)
                ye = em.op("vector", "tensor_tensor", Y[:, o * 512:(o + 1) * 512],
                           acc[:, :], xres[:, o * 512:(o + 1) * 512], ALU.add,
                           inc=[(c.vV, 1)])
                c.acc.release(ia, ye)
                stats_evt, sq_evt = emit_stats_mms(
                    c, Y[:, o * 512:(o + 1) * 512], ye, sq, sq_evt,
                    first=(o == 0), last=(o == DC - 1))
            ogblk_free = mme
            xres_free = ye
            A_t, B_t = emit_ln_finalize(c, ln_sc, stats_evt, 1)
            iA, Ab, evtA = A_t
            iB, Bb, evtB = B_t
            pe = None
            for o in range(DC):
                em.wait_evt("vector", evtA)
                em.wait_evt("vector", evtB)
                si = o % 2
                em.wait_evt("vector", scr_free[si])
                em.op("vector", "tensor_tensor", scr[si][:, :],
                      Y[:, o * 512:(o + 1) * 512], Ab[:, :], ALU.mult)
                p = o % 2
                em.wait_evt("vector", piece_free[p])
                pe = em.op("vector", "tensor_tensor", piece[p][:, :],
                           scr[si][:, :], Bb[:, :], ALU.add, inc=[(c.vV, 1)])
                scr_free[si] = pe
                em.wait_evt("sync", pe)
                de = em.op("sync", "dma_start",
                           x3_out[o * 128:(o + 1) * 128, nt * 512:(nt + 1) * 512],
                           piece[p][:, :], inc=[(c.dS, 16)])
                piece_free[p] = de
            Y_free = pe
            c.acc.release(iA, pe)
            c.acc.release(iB, pe)

        em.wait("sync", c.dS, c.dS.n)
        em.wait("gpsimd", c.dG, c.dG.n)
        em.replay()
    return nc


def attn_mask():
    qq = np.arange(128)[:, None]
    kk = np.arange(128)[None, :]
    return np.where(kk <= qq, 0.0, -1e30).astype(np.float32)



def build_launch_D():
    nc = bass.Bass("TRN2", target_bir_lowering=False, debug=False)
    stack = contextlib.ExitStack()

    x3 = nc.dram_tensor("x3", [D, T], BF16, kind="ExternalInput")
    Wup = nc.dram_tensor("Wup1", [INTER, D], F32, kind="ExternalInput")
    Wdn = nc.dram_tensor("Wdn1", [D, INTER], F32, kind="ExternalInput")
    UPS = nc.dram_tensor("UPS", [128, IC], F32, kind="ExternalInput")
    DNS = nc.dram_tensor("DNS", [128, DC], F32, kind="ExternalInput")
    OSC = nc.dram_tensor("OSC", [128, 1], F32, kind="ExternalInput")
    yT_out = nc.dram_tensor("yT", [D, T], F32, kind="ExternalOutput")

    hid_scr = nc.dram_tensor("hid_scr", [4 * INTER, 512], BF16, kind="Internal")
    wdT_scr = nc.dram_tensor("wd1T_scr", [DC * INTER, 128], BF16, kind="Internal")

    with stack:
        c = make_ctx(nc, stack)
        em = c.em

        AR_A = stack.enter_context(nc.sbuf_tensor("AR_A", [128, DC * D], BF16))
        AR_B2 = stack.enter_context(nc.sbuf_tensor("AR_B2", [128, 2 * INTER], BF16))
        AR_Y = stack.enter_context(nc.sbuf_tensor("AR_Y", [128, SEG + 2048], BF16))
        vecs = stack.enter_context(nc.sbuf_tensor("vecsD", [128, 2 * IC], F32))
        upsc = vecs[:, 0:IC]
        dns_sb = vecs[:, IC:IC + DC]
        dsos = vecs[:, IC + DC:IC + 2 * DC]
        osc_sb = vecs[:, IC + 2 * DC:IC + 2 * DC + 1]
        scr = [stack.enter_context(nc.sbuf_tensor(f"scrD{i}", [128, 512], F32))
               for i in range(2)]
        fpiece = [stack.enter_context(nc.sbuf_tensor(f"fp{i}", [128, 512], F32))
                  for i in range(2)]
        piece = [stack.enter_context(nc.sbuf_tensor(f"pieceD{i}", [128, 512], BF16))
                 for i in range(2)]
        sq = stack.enter_context(nc.sbuf_tensor("sqD", [128, 512], BF16))
        ln_sc = make_ln_scratch(nc, stack, "D")

        scr_free = [None, None]
        piece_free = [None, None]
        fpiece_free = [None, None]

        for v_sb, v_d in ((upsc, UPS), (dns_sb, DNS), (osc_sb, OSC)):
            vev = em.op("sync", "dma_start", v_sb, v_d[:, :], inc=[(c.dS, 16)])
        em.wait("vector", c.dS, c.dS.n)
        vv1 = em.op("vector", "tensor_scalar", dsos, dns_sb, osc_sb, None,
                    ALU.mult, inc=[(c.vV, 1)])

        x3_re = x3[:, :].rearrange("(c p) n -> p c n", p=128)
        x3ld = em.op("gpsimd", "dma_start",
                     AR_A[:, :].rearrange("p (c n) -> p c n", c=DC),
                     x3_re[:, :, :], inc=[(c.dG, 16)])

        natU = [AR_B2[:, i * D:(i + 1) * D] for i in range(2)]
        natS = AR_B2[:, 2 * D:3 * D]
        wtU = [AR_B2[:, (3 + i) * D:(4 + i) * D] for i in range(2)]
        natU_free = [None, None]
        natS_free = [None]
        wtU_free = [None, None]
        hp_free = [None, None]

        em.wait_evt("tensor", c.const_evt)
        for cc in range(IC):
            sl = cc % 2
            em.wait_evt("gpsimd", natU_free[sl])
            ld = em.op("gpsimd", "dma_start", natU[sl],
                       Wup[cc * 128:(cc + 1) * 128, :], inc=[(c.dG, 16)])
            em.wait_evt("scalar", ld)
            em.wait_evt("scalar", natS_free[0])
            sg = em.op("scalar", "activation", natS, natU[sl], AF.Sign,
                       inc=[(c.sA, 1)])
            natU_free[sl] = sg
            tev = None
            for g in range(4):
                bi, bt, bfree = c.ptr.acquire()
                em.wait_evt("tensor", sg)
                em.wait_evt("tensor", bfree)
                for i in range(4):
                    k = g * 4 + i
                    inc = [(c.tM, 1)] if i == 3 else None
                    tev = em.op("tensor", "transpose",
                                bt[:, i * 128:(i + 1) * 128],
                                natS[:, k * 128:(k + 1) * 128],
                                c.idn[:, :], inc=inc)
                em.wait_evt("vector", tev)
                em.wait_evt("vector", wtU_free[sl])
                evac = em.op("vector", "tensor_copy",
                             wtU[sl][:, g * 512:(g + 1) * 512],
                             bt[:, 0:512], inc=[(c.vV, 1)])
                c.ptr.release(bi, evac)
            natS_free[0] = tev
            for nt in range(NT):
                ia, acc, fr = c.acc.acquire()
                em.wait_evt("tensor", fr)
                em.wait_evt("tensor", evac)
                em.wait_evt("tensor", x3ld)
                for k in range(DC):
                    mme = em.op("tensor", "matmul", acc[:, :],
                                wtU[sl][:, k * 128:(k + 1) * 128],
                                AR_A[:, k * D + nt * 512:k * D + (nt + 1) * 512],
                                start=(k == 0), stop=(k == DC - 1),
                                inc=([(c.tM, 1)] if k == DC - 1 else None))
                p = nt % 2
                em.wait_evt("scalar", mme)
                em.wait_evt("scalar", hp_free[p])
                he = em.op("scalar", "activation", piece[p][:, :], acc[:, :],
                           AF.Relu, scale=upsc[:, cc:cc + 1], inc=[(c.sA, 1)])
                c.acc.release(ia, he)
                em.wait_evt("sync", he)
                hp_free[p] = em.op(
                    "sync", "dma_start",
                    hid_scr[nt * INTER + cc * 128:nt * INTER + (cc + 1) * 128, :],
                    piece[p][:, :], inc=[(c.dS, 16)])
            wtU_free[sl] = mme
        d1_mms_done = (c.tM, c.tM.n)

        nat8 = AR_B2[:, 0:INTER]
        natS8 = AR_B2[:, INTER:2 * INTER]
        nat8_free = d1_mms_done
        natS8_free = d1_mms_done
        wdp_free = [hp_free[0], hp_free[1]]
        for s in range(DC):
            em.wait_evt("gpsimd", nat8_free)
            em.wait("gpsimd", c.tM, c.tM.n)
            ld = em.op("gpsimd", "dma_start", nat8,
                       Wdn[s * 128:(s + 1) * 128, :], inc=[(c.dG, 16)])
            em.wait_evt("scalar", ld)
            em.wait_evt("scalar", natS8_free)
            sg = em.op("scalar", "activation", natS8, nat8, AF.Sign,
                       inc=[(c.sA, 1)])
            nat8_free = sg
            for g in range(16):
                bi, bt, bfree = c.ptr.acquire()
                em.wait_evt("tensor", sg)
                em.wait_evt("tensor", bfree)
                for i in range(4):
                    k = g * 4 + i
                    inc = [(c.tM, 1)] if i == 3 else None
                    tev = em.op("tensor", "transpose",
                                bt[:, i * 128:(i + 1) * 128],
                                natS8[:, k * 128:(k + 1) * 128],
                                c.idn[:, :], inc=inc)
                p = g % 2
                em.wait_evt("vector", tev)
                em.wait_evt("vector", wdp_free[p])
                ev = em.op("vector", "tensor_copy", piece[p][:, :], bt[:, 0:512],
                           inc=[(c.vV, 1)])
                c.ptr.release(bi, ev)
                em.wait_evt("sync", ev)
                dst = wdT_scr[s * INTER + g * 512:s * INTER + (g + 1) * 512, :]
                dst3 = dst.rearrange("(cc i) o -> i cc o", i=128)
                src3 = piece[p][:, :].rearrange("i (cc o) -> i cc o", cc=4)
                wdp_free[p] = em.op("sync", "dma_start", dst3, src3,
                                    inc=[(c.dS, 16)])
            natS8_free = tev
        piece_free = [wdp_free[0], wdp_free[1]]
        wd_prep_done = (c.dS, c.dS.n)

        hsup = AR_A
        wdT = [AR_B2[:, i * INTER:(i + 1) * INTER] for i in range(2)]
        Y = AR_Y[:, 0:SEG]
        x3p = [AR_Y[:, SEG + i * 512:SEG + (i + 1) * 512] for i in range(2)]
        hsup_free = d1_mms_done
        wdT_free = [natS8_free, natS8_free]
        Y_free = None
        x3p_free = [None, None]
        for sup in range(NT):
            em.wait("sync", *wd_prep_done)
            em.wait_evt("sync", hsup_free)
            hld = em.op("sync", "dma_start",
                        hsup[:, :].rearrange("p (cc n) -> p cc n", cc=IC),
                        hid_scr[sup * INTER:(sup + 1) * INTER, :]
                        .rearrange("(cc p) n -> p cc n", p=128),
                        inc=[(c.dS, 16)])
            stats_evt = None
            sq_evt = None
            for s in range(DC):
                sl = s % 2
                em.wait_evt("sync", wdT_free[sl])
                wld = em.op("sync", "dma_start",
                            wdT[sl].rearrange("i (cc o) -> i cc o", cc=IC),
                            wdT_scr[s * INTER:(s + 1) * INTER, :]
                            .rearrange("(cc i) o -> i cc o", i=128),
                            inc=[(c.dS, 16)])
                em.wait_evt("gpsimd", x3p_free[sl])
                x3ld2 = em.op("gpsimd", "dma_start", x3p[sl],
                              x3[s * 128:(s + 1) * 128,
                                 sup * 512:(sup + 1) * 512], inc=[(c.dG, 16)])
                ia, acc, fr = c.acc.acquire()
                em.wait_evt("tensor", fr)
                em.wait_evt("tensor", wld)
                em.wait_evt("tensor", hld)
                for cci in range(IC):
                    mme = em.op("tensor", "matmul", acc[:, :],
                                wdT[sl][:, cci * 128:(cci + 1) * 128],
                                hsup[:, cci * 512:(cci + 1) * 512],
                                start=(cci == 0), stop=(cci == IC - 1),
                                inc=([(c.tM, 1)] if cci == IC - 1 else None))
                wdT_free[sl] = mme
                em.wait_evt("vector", mme)
                em.wait_evt("vector", x3ld2)
                em.wait_evt("vector", Y_free)
                em.wait_evt("vector", vv1)
                ye = em.op("vector", "scalar_tensor_tensor",
                           Y[:, s * 512:(s + 1) * 512], acc[:, :],
                           dsos[:, s:s + 1], x3p[sl], ALU.mult, ALU.add,
                           inc=[(c.vV, 1)])
                c.acc.release(ia, ye)
                x3p_free[sl] = ye
                stats_evt, sq_evt = emit_stats_mms(
                    c, Y[:, s * 512:(s + 1) * 512], ye, sq, sq_evt,
                    first=(s == 0), last=(s == DC - 1))
            hsup_free = mme
            A_t, B_t = emit_ln_finalize(c, ln_sc, stats_evt, 1)
            iA, Ab, evtA = A_t
            iB, Bb, evtB = B_t
            pe = None
            for s in range(DC):
                em.wait_evt("vector", evtA)
                em.wait_evt("vector", evtB)
                si = s % 2
                em.wait_evt("vector", scr_free[si])
                em.op("vector", "tensor_tensor", scr[si][:, :],
                      Y[:, s * 512:(s + 1) * 512], Ab[:, :], ALU.mult)
                p = s % 2
                em.wait_evt("vector", fpiece_free[p])
                pe = em.op("vector", "tensor_tensor", fpiece[p][:, :],
                           scr[si][:, :], Bb[:, :], ALU.add, inc=[(c.vV, 1)])
                scr_free[si] = pe
                em.wait_evt("sync", pe)
                de = em.op("sync", "dma_start",
                           yT_out[s * 128:(s + 1) * 128,
                                  sup * 512:(sup + 1) * 512],
                           fpiece[p][:, :], inc=[(c.dS, 16)])
                fpiece_free[p] = de
            Y_free = pe
            c.acc.release(iA, pe)
            c.acc.release(iB, pe)

        em.wait("sync", c.dS, c.dS.n)
        em.wait("gpsimd", c.dG, c.dG.n)
        em.replay()
    return nc


def pmajor(v, nchunks):
    return np.ascontiguousarray(np.asarray(v).reshape(nchunks, 128).T)


def b_inputs(inputs):
    im = dict(const_inputs())
    im["Wup"] = inputs["L0_ffn_W_up"]
    im["Wgt"] = inputs["L0_ffn_W_gate"]
    im["Wdn"] = inputs["L0_ffn_W_down"]
    im["H2048"] = hadamard(2048).astype(bfloat16)
    im["H128"] = hadamard(128).astype(bfloat16)
    im["SUU"] = pmajor(inputs["L0_ffn_SU_ug"], DC)
    im["SUD"] = pmajor(inputs["L0_ffn_SU_dp"], IC)
    im["WSU"] = pmajor(inputs["L0_ffn_Wscale_ug"][:INTER], IC)
    im["WSG"] = pmajor(inputs["L0_ffn_Wscale_ug"][INTER:], IC)
    im["WSD"] = pmajor(inputs["L0_ffn_Wscale_dp"], DC)
    return im


def kernel(**inputs):
    x = inputs["L0_attn_x"]
    n = 8
    consts = const_inputs()
    cores = list(range(n))

    lm = la_masks()
    in_maps = []
    for b in range(n):
        im = dict(consts)
        im["xT"] = np.ascontiguousarray(x[b].T)
        im["Wq"] = inputs["L0_attn_Wq"]
        im["Wk"] = inputs["L0_attn_Wk"]
        im["Wv"] = inputs["L0_attn_Wv"]
        im["Wg"] = inputs["L0_attn_Wg"]
        im["Wp"] = inputs["L0_attn_Wp"]
        im["LMASK"] = lm
        in_maps.append(im)
    res = run_bass_kernel_spmd(build_launch_A(), in_maps, core_ids=cores)
    acts = [np.asarray(r["x1"]) for r in res.results]

    imb = b_inputs(inputs)
    in_maps = [{**imb, "x1": acts[b]} for b in range(n)]
    res = run_bass_kernel_spmd(build_launch_B(), in_maps, core_ids=cores)
    acts = [np.asarray(r["x2r"]) for r in res.results]

    imc = dict(consts)
    imc["Wq1"] = inputs["L1_attn_Wq"]
    imc["Wk1"] = inputs["L1_attn_Wk"]
    imc["Wv1"] = inputs["L1_attn_Wv"]
    imc["Wo1"] = inputs["L1_attn_Wo"]
    imc["AMASK"] = attn_mask()
    in_maps = [{**imc, "x2r": acts[b]} for b in range(n)]
    res = run_bass_kernel_spmd(build_launch_C(), in_maps, core_ids=cores)
    acts = [np.asarray(r["x3"]) for r in res.results]

    imd = dict(consts)
    imd["Wup1"] = inputs["L1_ffn_up_weight"]
    imd["Wdn1"] = inputs["L1_ffn_down_weight"]
    imd["UPS"] = pmajor(inputs["L1_ffn_up_scale"], IC)
    imd["DNS"] = pmajor(inputs["L1_ffn_down_scale"], DC)
    imd["OSC"] = np.ascontiguousarray(np.broadcast_to(
        np.asarray(inputs["L1_ffn_output_scale"], np.float32).reshape(1, 1),
        (128, 1)))
    in_maps = [{**imd, "x3": acts[b]} for b in range(n)]
    res = run_bass_kernel_spmd(build_launch_D(), in_maps, core_ids=cores)
    out = np.stack([np.asarray(r["yT"], dtype=np.float32).T
                    for r in res.results])
    return out
